# version 04
# baseline (speedup 1.0000x reference)
import numpy as np
from contextlib import ExitStack

import concourse.bass as bass
import concourse.mybir as mybir
from concourse.bass_utils import run_bass_kernel_spmd

F32 = mybir.dt.float32
BF16 = mybir.dt.bfloat16
AF = mybir.ActivationFunctionType
ALU = mybir.AluOpType

H = 2048
HP = H // 128
F = 8192
FP = F // 128
T = 4096
NB = T // 512
NQ = 4
EPS_LN = 1e-5
EPS_RMS = 1e-5
EPS_FFN = float(np.finfo(np.float32).eps)
INV_C = 1.0 / 0.501953125

ENGS = ("te", "ve", "ac", "gp", "sy")
DEBUG_OUTPUTS = False


class Prog:

    def __init__(self, nc):
        self.nc = nc
        self.thunks = {e: [] for e in ENGS}
        self.sems = {}
        self.cnt = {}
        self.waited = {e: {} for e in ENGS}
        self.ss_sem = {}
        self.ss_ctr = {"ve": 0, "ac": 0}
        self.marks = []

    def mark(self, name):
        self.marks.append((name, {e: len(self.thunks[e]) for e in ENGS}))

    def add_sem(self, name, handle):
        self.sems[name] = handle
        self.cnt[name] = 0

    def _needed_waits(self, eng, deps):
        out = []
        for d in deps:
            if d is None:
                continue
            s, v = d
            if self.waited[eng].get(s, 0) < v:
                out.append((s, v))
                self.waited[eng][s] = v
        return out

    def emit(self, eng, fn, deps=(), inc=None, inc_by=1, is_dma=False):
        waits = self._needed_waits(eng, deps)
        tok = None
        if inc is not None:
            self.cnt[inc] += inc_by
            tok = (inc, self.cnt[inc])
        sems = self.sems
        psync = inc is not None and eng in ("ve", "ac") and not is_dma

        def thunk(engine, waits=waits, fn=fn, inc=inc, inc_by=inc_by, psync=psync):
            for s, v in waits:
                engine.wait_ge(sems[s], v)
            last = fn(engine)
            if inc is not None:
                assert last is not None
                if psync:
                    ss = self.ss_sem[eng]
                    self.ss_ctr[eng] += 1
                    last.then_inc(ss, 1)
                    engine.wait_ge(ss, self.ss_ctr[eng])
                    last = engine.engine_nop() if eng == "ve" else engine.nop()
                last.then_inc(sems[inc], inc_by)
        self.thunks[eng].append(thunk)
        return tok

    def dma(self, qeng, out, in_, deps=(), **kw):
        qsem = {"sy": "qs", "ac": "qa", "gp": "qg"}[qeng]
        return self.emit(qeng, lambda e: e.dma_start(out=out, in_=in_, **kw),
                         deps=deps, inc=qsem, inc_by=16, is_dma=True)


def build_program():
    nc = bass.Bass("TRN2", target_bir_lowering=False)

    xT = nc.dram_tensor("xT", [H, T], F32, kind="ExternalInput").ap()
    wiT = nc.dram_tensor("wiT", [H, H], F32, kind="ExternalInput").ap()
    wfT = nc.dram_tensor("wfT", [H, H], F32, kind="ExternalInput").ap()
    wgT = nc.dram_tensor("wgT", [H, H], F32, kind="ExternalInput").ap()
    woT = nc.dram_tensor("woT", [H, H], F32, kind="ExternalInput").ap()
    wfcT = nc.dram_tensor("wfcT", [H, F], F32, kind="ExternalInput").ap()
    wprT = nc.dram_tensor("wprT", [F, H], F32, kind="ExternalInput").ap()
    gnw = nc.dram_tensor("gnw", [H], F32, kind="ExternalInput").ap()
    outT = nc.dram_tensor("outT", [H, T], F32, kind="ExternalOutput").ap()
    _K = "ExternalOutput" if DEBUG_OUTPUTS else "Internal"
    oSp = nc.dram_tensor("oSp", [128, HP, T], BF16, kind=_K).ap()
    gsSp = nc.dram_tensor("gsSp", [128, HP, T], BF16, kind=_K).ap()
    xmid = nc.dram_tensor("xmid", [128, HP, T], F32, kind=_K).ap()
    w1B = nc.dram_tensor("w1B", [H, F], BF16, kind=_K).ap()
    w2B = nc.dram_tensor("w2B", [F, H], BF16, kind=_K).ap()
    rowsD = nc.dram_tensor("rowsD", [4, T], F32, kind=_K).ap()
    scD = nc.dram_tensor("scD", [32, 512], BF16, kind="Internal").ap()

    es = ExitStack()
    with es:
        def sb(name, shape, dt):
            return es.enter_context(nc.sbuf_tensor(name, shape, dt))

        arA = sb("arA", [128, 8192], F32)
        arB = sb("arB", [128, 16384], F32)
        arC = sb("arC", [128, 8192], F32)
        arD = sb("arD", [128, 4096], F32)
        arE = sb("arE", [128, 2048], F32)
        arF = sb("arF", [128, 2048], F32)
        scanT = sb("scanT", [128, 128], BF16)
        scanS = sb("scanS", [128, 128], BF16)
        scanS2 = sb("scanS2", [128, 128], BF16)
        scanP = sb("scanP", [128, 128], BF16)
        scanP2 = sb("scanP2", [128, 128], BF16)
        scanC = sb("scanC", [128, 128], BF16)
        gsT = sb("gsT", [128, 1024], BF16)
        sqP0 = sb("sqP0", [128, 2048], BF16)
        wslA = sb("wslA", [128, 6144], BF16)
        wslB = sb("wslB", [128, 6144], BF16)
        ones1 = sb("ones1", [128, 1], BF16)
        e256 = sb("e256", [128, 256], BF16)
        gncol = sb("gncol", [128, 16], F32)
        qcar = sb("qcar", [128, 16], BF16)
        rowW = sb("rowW", [1, 3584], F32)

        psum = [es.enter_context(nc.psum_tensor(f"pb{i}", [128, 512], F32))
                for i in range(8)]

        P = Prog(nc)
        for name in ("qs", "qa", "qg", "te", "ve", "ac"):
            P.add_sem(name, es.enter_context(nc.semaphore(name)))
        vs_sem = es.enter_context(nc.semaphore("vs"))
        as_sem = es.enter_context(nc.semaphore("as"))
        P.ss_sem = {"ve": vs_sem, "ac": as_sem}

        def ssv(engine, inst):
            P.ss_ctr["ve"] += 1
            inst.then_inc(vs_sem, 1)
            engine.wait_ge(vs_sem, P.ss_ctr["ve"])
            return inst

        aA = arA[:].bitcast(BF16)
        aB = arB[:].bitcast(BF16)
        aD = arD[:].bitcast(BF16)
        aE = arE[:].bitcast(BF16)
        xf = arC

        def btile(i):
            return arF[:, i * 512:(i + 1) * 512]

        def W(i):
            return rowW[0:1, i * 512:(i + 1) * 512]

        def rbc(lane, blk):
            return rowsD[lane:lane + 1, blk * 512:(blk + 1) * 512] \
                .unsqueeze(1).broadcast_to([1, 128, 512])

        def sbc(w_ap):
            return w_ap.unsqueeze(1).broadcast_to([1, 128, 512])

        xTv = xT.rearrange("(c k) t -> k c t", k=128)
        outTv = outT.rearrange("(c k) t -> k c t", k=128)

        def r3(ap, m):
            return ap.rearrange("p (c m) -> p c m", m=m)

        P.mark("P0")
        def init_consts(e):
            e.memset(ones1[:], 1.0)
            e.memset(e256[:], 0.0)
            e.memset(e256[0:64, 126:127], 1.0)
            e.memset(e256[64:128, 127:128], 1.0)
            e.memset(qcar[:], 0.0)
            return None
        P.emit("ve", init_consts)
        t_gn = P.dma("gp", gncol[:], gnw.rearrange("(c k) -> k c", k=128),
                     allow_slow_non_contiguous=True)

        def emit_rsqrt(var_ap, out_ap, t1_ap, deps):
            ta = P.emit("ac", lambda e: e.activation(t1_ap, var_ap, AF.Sqrt),
                        deps=deps, inc="ac")

            def f(e):
                ssv(e, e.reciprocal(out_ap, t1_ap))
                ssv(e, e.tensor_mul(t1_ap, out_ap, out_ap))
                ssv(e, e.tensor_mul(t1_ap, t1_ap, var_ap))
                ssv(e, e.tensor_scalar(t1_ap, t1_ap, -0.5, 1.5, ALU.mult, ALU.add))
                ssv(e, e.tensor_mul(out_ap, out_ap, t1_ap))
                return e.engine_nop()
            return P.emit("ve", f, deps=[ta], inc="ve")

        xb = aD
        t_mm_prev = None
        t_ev_prev = None
        t_xld_prev = None
        t_rowsD1 = [None] * NB
        for blk in range(NB):
            t_xb = P.dma("gp", r3(xb[:, 0:8192], 512),
                         xTv[:, :, blk * 512:(blk + 1) * 512],
                         deps=[t_mm_prev])
            p0_mm = {}
            for g in range(8):
                hh = g % 2
                t_sqh = P.emit(
                    "ac",
                    lambda e, g=g, hh=hh: e.activation(
                        sqP0[:, hh * 1024:(hh + 1) * 1024],
                        xb[:, g * 1024:(g + 1) * 1024], AF.Square),
                    deps=[t_xb, p0_mm.get(g - 2, t_mm_prev)], inc="ac")

                def mk_mm(g=g, hh=hh):
                    def f(e):
                        last = None
                        for c in range(2):
                            pt = g * 2 + c
                            e.matmul(psum[0][0:1, :], ones1[:],
                                     xb[:, pt * 512:(pt + 1) * 512],
                                     start=(pt == 0), stop=(pt == 15))
                            last = e.matmul(psum[1][0:1, :], ones1[:],
                                            sqP0[:, hh * 1024 + c * 512:
                                                 hh * 1024 + (c + 1) * 512],
                                            start=(pt == 0), stop=(pt == 15))
                        return last
                    return f
                t_mm_prev = P.emit("te", mk_mm(),
                                   deps=[t_sqh, t_ev_prev if g == 0 else None],
                                   inc="te")
                p0_mm[g] = t_mm_prev
            def p0rows(e):
                ssv(e, e.tensor_scalar_mul(W(0), psum[0][0:1, :], 1.0 / H))
                ssv(e, e.tensor_scalar_mul(W(1), psum[1][0:1, :], 1.0 / H))
                ssv(e, e.tensor_mul(W(4), W(0), W(0)))
                ssv(e, e.tensor_sub(W(1), W(1), W(4)))
                ssv(e, e.tensor_scalar_add(W(1), W(1), EPS_LN))
                return e.engine_nop()
            t_ev = P.emit("ve", p0rows,
                deps=[t_mm_prev, t_rowsD1[blk - 1] if blk else None], inc="ve")
            t_ev_prev = t_ev
            t_r1 = emit_rsqrt(W(1), W(2), W(4), [t_ev])
            t_mr = P.emit("ve", lambda e: (ssv(e, e.tensor_mul(W(3), W(0), W(2))),
                                           e.engine_nop())[-1],
                          deps=[t_r1], inc="ve")
            tok_a = P.dma("ac", rowsD[0:1, blk * 512:(blk + 1) * 512], W(2),
                          deps=[t_r1])
            tok_b = P.dma("ac", rowsD[1:2, blk * 512:(blk + 1) * 512], W(3),
                          deps=[t_mr])
            t_rowsD1[blk] = tok_b
        t_rstd1 = t_rowsD1[NB - 1]
        t_mr1 = t_rstd1

        P.mark("A")
        x1q = aA
        gate = aB
        vA = aB
        wslT = (wslA[:], wslB[:])
        t_x1_blk = [None, None]
        t_x1_last = None
        t_lastmm = None
        t_spill_o = [None, None]
        t_scan_grp = [None, None]
        t_gs_spill = None
        t_bank = {}
        t_tmpwar = {0: None, 1: None}
        t_slotmm = {}
        t_nblk_mm = {}
        scan_pending = []
        scan_last = [None]

        def drain_scan(k):
            while k > 0 and scan_pending:
                item = scan_pending.pop(0)
                if item[0] == "task":
                    scan_last[0] = P.emit("ve", item[1], deps=item[2], inc="ve")
                else:
                    _, g8, qq = item
                    t_spill_o[g8] = P.dma(
                        "ac", oSp[:, g8 * 8:(g8 + 1) * 8, qq * 1024:(qq + 1) * 1024],
                        vA[:, 16384 + g8 * 8192: 16384 + (g8 + 1) * 8192]
                        .rearrange("p (pt t) -> p pt t", t=1024),
                        deps=[scan_last[0]])
                k -= 1
        wsrcs = (wfT, wiT, wgT)

        for q in range(NQ):
            x1_toks = []
            for n in range(2):
                blk = q * 2 + n
                t_x = P.dma("sy", r3(xf[:, :], 512),
                            xTv[:, :, blk * 512:(blk + 1) * 512],
                            deps=[x1_toks[-1] if x1_toks else t_x1_last,
                                  t_rowsD1[blk] if q == 0 else None])
                t_b1 = P.dma("ac", btile(n * 2), rbc(0, blk),
                             deps=[t_rowsD1[blk], t_x1_last])
                t_b2 = P.dma("ac", btile(n * 2 + 1), rbc(1, blk),
                             deps=[t_rowsD1[blk], t_x1_last])

                def mk_x1(n):
                    def f(e):
                        last = None
                        for pt in range(HP):
                            e.tensor_mul(psum[6][:], xf[:, pt * 512:(pt + 1) * 512],
                                         btile(n * 2))
                            last = e.tensor_sub(
                                x1q[:, pt * 1024 + n * 512: pt * 1024 + n * 512 + 512],
                                psum[6][:], btile(n * 2 + 1))
                        return last
                    return f
                tok = P.emit("ve", mk_x1(n), deps=[t_x, t_b1, t_b2,
                                                   t_nblk_mm.get(n)],
                             inc="ve")
                x1_toks.append(tok)
                t_x1_blk[n] = tok
                t_x1_last = tok

            def load_wsl(po):
                slot = po % 2
                toks = []
                for wi, wsrc in enumerate(wsrcs):
                    toks.append(P.dma(
                        "gp", r3(wslT[slot][:, wi * 2048:(wi + 1) * 2048], 128),
                        wsrc[:, po * 128:(po + 1) * 128]
                        .rearrange("(c k) m -> k c m", k=128),
                        deps=[t_slotmm.get(slot)]))
                return toks
            if q == 0:
                wsl_next = load_wsl(0)
            for po in range(HP):
                grp = po // 8
                wsl_toks = wsl_next
                if po < HP - 1:
                    wsl_next = load_wsl(po + 1)
                elif q < NQ - 1:
                    wsl_next = load_wsl(0)
                for n in range(2):
                    bf_, bi_, bg_ = n % 2, 2 + n % 2, 4 + n % 2

                    def mk_fig(n=n, bf_=bf_, bi_=bi_, bg_=bg_, slot=po % 2):
                        def f(e):
                            last = None
                            for wi, bank in ((0, bf_), (1, bi_), (2, bg_)):
                                for c in range(HP):
                                    last = e.matmul(
                                        psum[bank][:],
                                        wslT[slot][:, wi * 2048 + c * 128:
                                                   wi * 2048 + (c + 1) * 128],
                                        x1q[:, c * 1024 + n * 512:
                                            c * 1024 + n * 512 + 512],
                                        start=(c == 0), stop=(c == HP - 1))
                            return last
                        return f
                    t_mm = P.emit("te", mk_fig(),
                                  deps=wsl_toks + [t_x1_blk[n], t_bank.get(bf_),
                                                   t_bank.get(bi_), t_bank.get(bg_)],
                                  inc="te")
                    t_lastmm = t_mm
                    t_nblk_mm[n] = t_mm
                    gm_ = aE[:, (n * 4 + 0) * 512:(n * 4 + 1) * 512]
                    sgi_ = aE[:, (n * 4 + 1) * 512:(n * 4 + 2) * 512]
                    sgg_ = aE[:, (n * 4 + 2) * 512:(n * 4 + 3) * 512]
                    si_ = aE[:, (n * 4 + 3) * 512:(n * 4 + 4) * 512]
                    g_dst = gate[:, po * 1024 + n * 512: po * 1024 + n * 512 + 512]
                    v_dst = vA[:, 16384 + po * 1024 + n * 512:
                               16384 + po * 1024 + n * 512 + 512]

                    def mk_act(bf_=bf_, bi_=bi_, bg_=bg_, g_dst=g_dst,
                               gm_=gm_, sgi_=sgi_, sgg_=sgg_):
                        def f(e):
                            e.activation(g_dst, psum[bf_][:], AF.Sigmoid)
                            e.activation(gm_, psum[bf_][:], AF.Sigmoid, scale=-1.0)
                            e.activation(sgi_, psum[bi_][:], AF.Sigmoid)
                            return e.activation(sgg_, psum[bg_][:], AF.Sigmoid)
                        return f
                    t_act = P.emit("ac", mk_act(),
                                   deps=[t_mm, t_spill_o[grp], t_tmpwar[n]],
                                   inc="ac")
                    t_bank[bf_] = t_act

                    def mk_vep(bi_=bi_, bg_=bg_, si_=si_, sgi_=sgi_, gm_=gm_,
                               sgg_=sgg_, v_dst=v_dst, n=n):
                        def f(e):
                            e.tensor_mul(si_, sgi_, psum[bi_][:])
                            e.tensor_mul(v_dst, si_, gm_)
                            return e.tensor_mul(gsT[:, n * 512:(n + 1) * 512],
                                                sgg_, psum[bg_][:])
                        return f
                    t_vep = P.emit("ve", mk_vep(),
                                   deps=[t_act, t_spill_o[grp], t_gs_spill],
                                   inc="ve")
                    t_bank[bi_] = t_vep
                    t_bank[bg_] = t_vep
                    t_tmpwar[n] = t_vep
                    t_bank[("g", po, n)] = t_act
                    t_bank[("v", po, n)] = t_vep
                    drain_scan(1)
                t_gs_spill = P.dma("ac", gsSp[:, po, q * 1024:(q + 1) * 1024],
                                   gsT[:, 0:1024], deps=[t_bank[5], t_bank[4]])
                t_slotmm[po % 2] = t_lastmm

                if po % 8 == 7:
                    g8 = grp
                    deps0 = []
                    for pp in range(g8 * 8, g8 * 8 + 8):
                        for n in range(2):
                            deps0 += [t_bank[("g", pp, n)], t_bank[("v", pp, n)]]
                    gate_g = gate[:, g8 * 8192:(g8 + 1) * 8192] \
                        .rearrange("p (pt m j) -> p pt m j", m=16, j=64)
                    v_g = vA[:, 16384 + g8 * 8192: 16384 + (g8 + 1) * 8192] \
                        .rearrange("p (pt m j) -> p pt m j", m=16, j=64)
                    tS = scanT[:].rearrange("p (pt m) -> p pt m", m=16)

                    def mk_l1(j0, j1, gate_g=gate_g, v_g=v_g):
                        def f(e):
                            last = None
                            for j in range(j0, j1):
                                e.tensor_mul(tS, gate_g[:, :, :, j],
                                             v_g[:, :, :, j - 1])
                                e.tensor_add(v_g[:, :, :, j], tS, v_g[:, :, :, j])
                                last = e.tensor_mul(gate_g[:, :, :, j],
                                                    gate_g[:, :, :, j],
                                                    gate_g[:, :, :, j - 1])
                            return last
                        return f

                    def mk_l2(gate_g=gate_g, v_g=v_g, g8=g8):
                        def f(e):
                            S = scanS[:].rearrange("p (pt m) -> p pt m", m=16)
                            S2 = scanS2[:].rearrange("p (pt m) -> p pt m", m=16)
                            Pp = scanP[:].rearrange("p (pt m) -> p pt m", m=16)
                            P2 = scanP2[:].rearrange("p (pt m) -> p pt m", m=16)
                            Ci = scanC[:].rearrange("p (pt m) -> p pt m", m=16)
                            e.tensor_copy(S, v_g[:, :, :, 63])
                            e.tensor_copy(Pp, gate_g[:, :, :, 63])
                            cur, nxt = (S, Pp), (S2, P2)
                            for s in (1, 2, 4, 8):
                                cS, cP = cur
                                nS, nP = nxt
                                ssv(e, e.tensor_mul(nS[:, :, s:], cP[:, :, s:],
                                                    cS[:, :, :16 - s]))
                                ssv(e, e.tensor_add(nS[:, :, s:], nS[:, :, s:],
                                                    cS[:, :, s:]))
                                ssv(e, e.tensor_copy(nS[:, :, 0:s], cS[:, :, 0:s]))
                                ssv(e, e.tensor_mul(nP[:, :, s:], cP[:, :, s:],
                                                    cP[:, :, :16 - s]))
                                ssv(e, e.tensor_copy(nP[:, :, 0:s], cP[:, :, 0:s]))
                                cur, nxt = nxt, cur
                            cS, cP = cur
                            nS, _ = nxt
                            for pp in range(8):
                                ssv(e, e.scalar_tensor_tensor(
                                    nS[:, pp, :], cP[:, pp, :],
                                    qcar[:, g8 * 8 + pp: g8 * 8 + pp + 1],
                                    cS[:, pp, :], ALU.mult, ALU.add))
                            ssv(e, e.tensor_copy(Ci[:, :, 0:1],
                                qcar[:, g8 * 8: g8 * 8 + 8].unsqueeze(2)))
                            ssv(e, e.tensor_copy(Ci[:, :, 1:16], nS[:, :, 0:15]))
                            ssv(e, e.tensor_copy(qcar[:, g8 * 8: g8 * 8 + 8],
                                                 nS[:, :, 15]))
                            return e.engine_nop()
                        return f

                    def mk_apply(j0, j1, gate_g=gate_g, v_g=v_g):
                        Ci = scanC[:].rearrange("p (pt m) -> p pt m", m=16)

                        def f(e):
                            last = None
                            for j in range(j0, j1):
                                e.tensor_mul(tS, gate_g[:, :, :, j], Ci)
                                last = e.tensor_add(v_g[:, :, :, j],
                                                    v_g[:, :, :, j], tS)
                            return last
                        return f

                    first = True
                    for j0 in range(1, 64, 9):
                        scan_pending.append(
                            ("task", mk_l1(j0, min(j0 + 9, 64)),
                             deps0 if first else []))
                        first = False
                    scan_pending.append(("task", mk_l2(), []))
                    for j0 in range(0, 64, 16):
                        scan_pending.append(("task", mk_apply(j0, j0 + 16), []))
                    scan_pending.append(("spill", g8, q))

        drain_scan(10 ** 9)
        tA = [t_lastmm, t_spill_o[0], t_spill_o[1], t_gs_spill]

        P.mark("B")
        oB = aA
        gsB = aA
        woS = aB
        o2T = aD
        t_wo = P.dma("gp", r3(woS[:, 0:32768], 2048),
                     woT.rearrange("(c k) m -> k c m", k=128), deps=tA)
        t_attn_bank = {2: None, 3: None}
        t_sqev_prev = None
        t_sq_prev = None
        t_o2_prev = None
        t_xm_spill = None
        t_ln2_ev_prev = None
        t_mm2_prev = None
        t_cst_prev = None
        t_b1_prev = None
        t_rowsD2 = [None] * NB
        sq_mm_toks = [None] * HP
        mm2_toks = {}
        sq2_toks = {}
        for blk in range(NB):
            mm2_toks.clear()
            t_o = P.dma("sy", r3(oB[:, 0:8192], 512),
                        oSp[:, :, blk * 512:(blk + 1) * 512],
                        deps=tA + [t_o2_prev, t_sq_prev])
            t_gs = P.dma("sy", r3(gsB[:, 8192:16384], 512),
                         gsSp[:, :, blk * 512:(blk + 1) * 512],
                         deps=[t_o2_prev])
            t_x = P.dma("sy", r3(xf[:, :], 512), xTv[:, :, blk * 512:(blk + 1) * 512],
                        deps=[t_x1_last, t_xm_spill])

            def mk_o2():
                def f(e):
                    last = None
                    for pt in range(HP):
                        last = e.scalar_tensor_tensor(
                            o2T[:, pt * 512:(pt + 1) * 512],
                            oB[:, pt * 512:(pt + 1) * 512],
                            gncol[:, pt:pt + 1],
                            gsB[:, 8192 + pt * 512: 8192 + (pt + 1) * 512],
                            ALU.mult, ALU.mult)
                    return last
                return f
            t_o2 = P.emit("ve", mk_o2(),
                          deps=[t_o, t_gs, t_gn, t_lastmm,
                                t_attn_bank[2], t_attn_bank[3]], inc="ve")
            t_o2_prev = t_o2
            sq_tiles = (aE[:, 0:512], aE[:, 512:1024], aE[:, 1024:1536],
                        aE[:, 1536:2048])
            sq_toks = [None] * HP
            mm_last = None
            for pt in range(HP + 3):
                if pt < HP:
                    d = sq_tiles[pt % 4]
                    sq_toks[pt] = P.emit(
                        "ac", lambda e, p=pt, d=d:
                        e.activation(d, oB[:, p * 512:(p + 1) * 512], AF.Square),
                        deps=[t_o, t_sq_prev if pt < 4 else sq_mm_toks[pt - 4],
                              t_mm2_prev if pt < 4 else None],
                        inc="ac")
                if pt >= 3:
                    p = pt - 3
                    mm_last = P.emit(
                        "te", lambda e, p=p, d=sq_tiles[p % 4]:
                        e.matmul(psum[6][0:1, :], ones1[:], d,
                                 start=(p == 0), stop=(p == HP - 1)),
                        deps=[sq_toks[p], (t_sqev_prev if p == 0 else None)],
                        inc="te")
                    sq_mm_toks[p] = mm_last
            t_sq_prev = mm_last
            t_sqev = P.emit("ve", lambda e:
                            (ssv(e, e.tensor_scalar(W(5), psum[6][0:1, :],
                                                    1.0 / H, EPS_RMS,
                                                    ALU.mult, ALU.add)),
                             e.engine_nop())[-1],
                            deps=[mm_last, t_b1_prev], inc="ve")
            t_sqev_prev = t_sqev
            t_rstd = emit_rsqrt(W(5), W(6), W(4), [t_sqev])
            t_b1 = P.dma("ac", btile(0), sbc(W(6)),
                         deps=[t_rstd, t_attn_bank[2], t_attn_bank[3]])
            t_b1_prev = t_b1
            for j in range(HP):
                bank = 2 + j % 2

                def mk_att(j=j, bank=bank):
                    def f(e):
                        last = None
                        for c in range(HP):
                            last = e.matmul(
                                psum[bank][:],
                                woS[:, c * 2048 + j * 128: c * 2048 + (j + 1) * 128],
                                o2T[:, c * 512:(c + 1) * 512],
                                start=(c == 0), stop=(c == HP - 1))
                        return last
                    return f
                t_mm = P.emit("te", mk_att(),
                              deps=[t_o2, t_wo, t_attn_bank[bank]], inc="te")

                def mk_ev(j=j, bank=bank):
                    def f(e):
                        e.tensor_mul(psum[7][:], psum[bank][:], btile(0))
                        return e.tensor_add(xf[:, j * 512:(j + 1) * 512],
                                            psum[7][:],
                                            xf[:, j * 512:(j + 1) * 512])
                    return f
                t_evj = P.emit("ve", mk_ev(), deps=[t_mm, t_x, t_b1], inc="ve")
                t_attn_bank[bank] = t_evj

                def emit_mm2(jj):
                    t_mm2 = P.emit("te", lambda e, j=jj, s=sq_tiles[jj % 2],
                                   d=sq_tiles[2 + jj % 2]: (
                        e.matmul(psum[4][0:1, :], ones1[:], s,
                                 start=(j == 0), stop=(j == HP - 1)),
                        e.matmul(psum[5][0:1, :], ones1[:], d,
                                 start=(j == 0), stop=(j == HP - 1)))[-1],
                        deps=[sq2_toks[jj],
                              (t_ln2_ev_prev if jj == 0 else None)], inc="te")
                    mm2_toks[jj] = t_mm2
                    return t_mm2

                if j >= 2:
                    t_mm2_prev = emit_mm2(j - 2)
                xm_t = sq_tiles[j % 2]
                sq_t = sq_tiles[2 + j % 2]
                t_cst = P.emit("ve", lambda e, j=j, d=xm_t:
                               e.tensor_copy(d, xf[:, j * 512:(j + 1) * 512]),
                               deps=[t_evj, mm2_toks.get(j - 2),
                                     (mm_last if j < 2 else None)], inc="ve")
                t_cst_prev = t_cst
                t_sq2 = P.emit("ac", lambda e, s=xm_t, d=sq_t:
                               e.activation(d, s, AF.Square),
                               deps=[t_cst, mm2_toks.get(j - 2),
                                     (mm_last if j < 2 else None)], inc="ac")
                sq2_toks[j] = t_sq2
                if j == HP - 1:
                    t_mm2_prev = emit_mm2(j - 1)
                    t_mm2_prev = emit_mm2(j)
            def ln2rows(e):
                ssv(e, e.tensor_scalar_mul(W(0), psum[4][0:1, :], 1.0 / H))
                ssv(e, e.tensor_scalar_mul(W(1), psum[5][0:1, :], 1.0 / H))
                ssv(e, e.tensor_mul(W(4), W(0), W(0)))
                ssv(e, e.tensor_sub(W(1), W(1), W(4)))
                ssv(e, e.tensor_scalar_add(W(1), W(1), EPS_LN))
                return e.engine_nop()
            t_ln2_ev = P.emit("ve", ln2rows,
                deps=[t_mm2_prev, t_rowsD2[blk - 1] if blk else t_rowsD1[NB - 1]],
                inc="ve")
            t_ln2_ev_prev = t_ln2_ev
            t_r2 = emit_rsqrt(W(1), W(2), W(4), [t_ln2_ev])
            t_mr2b = P.emit("ve", lambda e: (ssv(e, e.tensor_copy(W(3), W(0))),
                                             e.engine_nop())[-1],
                            deps=[t_r2], inc="ve")
            tok_a = P.dma("ac", rowsD[2:3, blk * 512:(blk + 1) * 512], W(2),
                          deps=[t_r2])
            tok_b = P.dma("ac", rowsD[3:4, blk * 512:(blk + 1) * 512], W(3),
                          deps=[t_mr2b])
            t_rowsD2[blk] = tok_b
            t_xm_spill = P.dma("ac", xmid[:, :, blk * 512:(blk + 1) * 512],
                               r3(xf[:, :], 512),
                               deps=[t_attn_bank[2], t_attn_bank[3], t_cst_prev])
        t_rstd2 = t_rowsD2[NB - 1]
        t_mr2 = t_rstd2

        P.mark("Q")
        aC = arC[:].bitcast(BF16)
        panels = []
        for pg in range(HP // 16):
            for cb in range(F // 512):
                panels.append((wfcT, w1B, F, pg, cb))
        for pg in range(FP // 16):
            for cb in range(H // 512):
                panels.append((wprT, w2B, H, pg, cb))
        nP = len(panels)
        scRow = aE[0:32, 0:512]
        dep0 = [t_xm_spill, t_sq_prev, t_mm2_prev]
        q_abs = [None] * nP
        q_sgn = [None] * nP
        q_gsm = [None] * nP
        q_ev = [None] * nP
        q_bc = [None] * nP
        q_ms = [None] * nP
        q_w1 = [None] * nP
        q_st = [None] * nP
        q_scw = [None] * nP

        def q_stage1(i):
            srcT, dstB, nCols, pg, cb = panels[i]
            s = i % 2
            wb_t = aB[:, 16384 + s * 8192: 16384 + (s + 1) * 8192]
            awb_t = aC[:, s * 8192:(s + 1) * 8192]
            sgn_t = aA[:, s * 8192:(s + 1) * 8192]
            colsl = slice(cb * 512, (cb + 1) * 512)
            t_wb = P.dma("gp", r3(wb_t, 512),
                         srcT[pg * 2048:(pg + 1) * 2048, colsl]
                         .rearrange("(c k) m -> k c m", k=128),
                         deps=dep0 + [q_sgn[i - 2] if i >= 2 else None])
            q_abs[i] = P.emit("ac", lambda e, d=awb_t, sc=wb_t:
                              e.activation(d, sc, AF.Abs),
                              deps=dep0 + [t_wb,
                                           q_st[i - 2] if i >= 2 else None],
                              inc="ac")
            q_sgn[i] = P.emit("ac", lambda e, d=sgn_t, sc=wb_t:
                              e.activation(d, sc, AF.Sign),
                              deps=[t_wb, q_w1[i - 2] if i >= 2 else None],
                              inc="ac")

            def mk_gs(awb_t=awb_t):
                def f(e):
                    last = None
                    for p in range(16):
                        sx = 126 - 2 * p
                        last = e.matmul(psum[7][0:32, :], e256[:, sx:sx + 32],
                                        awb_t[:, p * 512:(p + 1) * 512],
                                        start=(p == 0), stop=(p == 15))
                    return last
                return f
            q_gsm[i] = P.emit("te", mk_gs(),
                              deps=[q_abs[i], q_ev[i - 1] if i else None],
                              inc="te")

        def q_stage2(i):
            s = i % 2
            scb_t = aB[:, s * 8192:(s + 1) * 8192]
            q_ev[i] = P.emit("ve", lambda e:
                             (ssv(e, e.tensor_scalar(scRow, psum[7][0:32, :],
                                                     1.0 / 64.0, 1e-8,
                                                     ALU.mult, ALU.max)),
                              e.engine_nop())[-1],
                             deps=dep0 + [q_gsm[i],
                                          q_scw[i - 1] if i else None], inc="ve")
            q_scw[i] = P.dma("ac", scD[:, :], scRow,
                             deps=[q_ev[i], q_bc[i - 1] if i else None])
            bcs = []
            for hh in range(2):
                srcv = scD.rearrange("(p h) m -> h p m", h=2)[hh] \
                    .unsqueeze(0).broadcast_to([64, 16, 512])
                bcs.append(P.dma("gp", scb_t[64 * hh:64 * hh + 64, :], srcv,
                                 deps=dep0 + [q_scw[i],
                                              q_ms[i - 2] if i >= 2 else None]))
            q_bc[i] = bcs[-1]

        def q_stage3(i):
            srcT, dstB, nCols, pg, cb = panels[i]
            s = i % 2
            awb_t = aC[:, s * 8192:(s + 1) * 8192]
            sgn_t = aA[:, s * 8192:(s + 1) * 8192]
            scb_t = aB[:, s * 8192:(s + 1) * 8192]
            colsl = slice(cb * 512, (cb + 1) * 512)
            t_mask = P.emit("ve", lambda e, a=awb_t, sc=scb_t:
                            e.scalar_tensor_tensor(a, a, INV_C, sc,
                                                   ALU.mult, ALU.is_gt),
                            deps=[q_bc[i]], inc="ve")
            q_ms[i] = P.emit("ve", lambda e, a=awb_t, sc=scb_t:
                             e.tensor_mul(a, a, sc),
                             deps=[t_mask], inc="ve")
            q_w1[i] = P.emit("ve", lambda e, a=awb_t, g=sgn_t:
                             e.tensor_mul(a, a, g),
                             deps=[q_ms[i], q_sgn[i]], inc="ve")
            q_st[i] = P.dma("ac",
                            dstB[pg * 2048:(pg + 1) * 2048, colsl]
                            .rearrange("(c k) m -> k c m", k=128),
                            r3(awb_t, 512), deps=[q_w1[i]])

        for i in range(nP):
            q_stage1(i)
            q_stage2(i)
            if i >= 1:
                q_stage3(i - 1)
        q_stage3(nP - 1)
        toks_w1 = [q_st[HP // 16 * (F // 512) - 1]]
        toks_w2 = [q_st[nP - 1]]

        P.mark("C")
        x2T = aD
        harea = aB
        w2S = aA[:, 0:8192]
        w2S2 = aA[:, 8192:16384]
        rT, rT2 = aE[:, 0:512], aE[:, 512:1024]
        sqC1, sqC2 = aE[:, 1024:1536], aE[:, 1536:2048]
        t_h_bank = {0: None, 1: None, 2: None}
        t_proj_bank = {4: None, 5: None}
        t_projmm_prev = None
        t_w1sl = [None, None, None]
        t_w2sl = [None, None]
        t_x2_prev = None
        t_hs_prev = None
        t_out_prev = None
        t_bF_prev = None
        t_rT_war = {0: None, 1: None}
        t_sqC_war = {0: None, 1: None}
        sq4_toks = {}
        t_hmm_last = None
        for ch in range(NB):
            colsl = slice(ch * 512, (ch + 1) * 512)
            t_xmb = P.dma("gp", r3(x2T[:, 0:8192], 512), xmid[:, :, colsl],
                          deps=[toks_w2[-1] if ch == 0 else None, t_hmm_last])
            t_xm = P.dma("sy", r3(xf[:, :], 512), xmid[:, :, colsl],
                         deps=[toks_w2[-1] if ch == 0 else None, t_out_prev])
            t_b1 = P.dma("ac", btile(0), rbc(2, ch),
                         deps=[t_rowsD2[ch], t_x2_prev])
            t_b2 = P.dma("ac", btile(1), rbc(3, ch),
                         deps=[t_rowsD2[ch], t_x2_prev])

            def mk_x2():
                def f(e):
                    last = None
                    for pt in range(HP):
                        sl_ = x2T[:, pt * 512:(pt + 1) * 512]
                        e.tensor_sub(sl_, sl_, btile(1))
                        last = e.tensor_mul(sl_, sl_, btile(0))
                    return last
                return f
            t_x2 = P.emit("ve", mk_x2(),
                          deps=[t_xmb, t_b1, t_b2,
                                (toks_w2[-1] if ch == 0 else None), t_x2_prev],
                          inc="ve")
            t_x2_prev = t_x2

            mm_sq_last = None
            for qo in range(FP):
                sl = qo % 3
                t_wl = P.dma("sy", r3(wslA[:, sl * 2048:(sl + 1) * 2048], 128),
                             w1B[:, qo * 128:(qo + 1) * 128]
                             .rearrange("(c k) m -> k c m", k=128),
                             deps=[t_w1sl[sl], t_lastmm,
                                   (toks_w1[-1] if (ch == 0 and qo < 3) else None)])
                bank = qo % 3

                def mk_h(sl=sl, bank=bank):
                    def f(e):
                        last = None
                        for c in range(HP):
                            last = e.matmul(
                                psum[bank][:],
                                wslA[:, sl * 2048 + c * 128: sl * 2048 + (c + 1) * 128],
                                x2T[:, c * 512:(c + 1) * 512],
                                start=(c == 0), stop=(c == HP - 1))
                        return last
                    return f
                t_hmm = P.emit("te", mk_h(), deps=[t_wl, t_x2, t_h_bank[bank]],
                               inc="te")
                t_w1sl[sl] = t_hmm
                t_hmm_last = t_hmm

                def emit_msq(q):
                    dq = sqC1 if q % 2 == 0 else sqC2
                    tok = P.emit("te", lambda e, q=q, dq=dq:
                                 e.matmul(psum[3][0:1, :], ones1[:], dq,
                                          start=(q == 0), stop=(q == FP - 1)),
                                 deps=[sq4_toks[q],
                                       (t_hs_prev if q == 0 else None)],
                                 inc="te")
                    t_sqC_war[q % 2] = tok
                    return tok

                if qo >= 2:
                    mm_sq_last = emit_msq(qo - 2)
                d = rT if qo % 2 == 0 else rT2
                t_r = P.emit("ac", lambda e, b=bank, d=d:
                             e.activation(d, psum[b][:], AF.Relu),
                             deps=[t_hmm, t_rT_war[qo % 2]], inc="ac")
                t_h_bank[bank] = t_r
                t_hsq = P.emit("ve", lambda e, q=qo, d=d:
                               e.tensor_mul(harea[:, q * 512:(q + 1) * 512], d, d),
                               deps=[t_r, (t_projmm_prev if qo == 0 else None),
                                     (toks_w2[-1] if (ch == 0 and qo == 0) else None)],
                               inc="ve")
                t_rT_war[qo % 2] = t_hsq
                dq = sqC1 if qo % 2 == 0 else sqC2
                t_sq4 = P.emit("ac", lambda e, q=qo, dq=dq:
                               e.activation(dq, harea[:, q * 512:(q + 1) * 512],
                                            AF.Square),
                               deps=[t_hsq, t_sqC_war[qo % 2]], inc="ac")
                sq4_toks[qo] = t_sq4
                if qo == FP - 1:
                    emit_msq(qo - 1)
                    mm_sq_last = emit_msq(qo)
            t_ev = P.emit("ve", lambda e:
                          (ssv(e, e.tensor_scalar(W(5), psum[3][0:1, :],
                                                  1.0 / F, EPS_FFN,
                                                  ALU.mult, ALU.add)),
                           e.engine_nop())[-1],
                          deps=[mm_sq_last, t_bF_prev], inc="ve")
            t_hs_prev = t_ev
            t_rstdF = emit_rsqrt(W(5), W(6), W(4), [t_ev])
            t_bF = P.dma("ac", btile(2), sbc(W(6)), deps=[t_rstdF])
            t_bF_prev = t_bF
            for j in range(HP):
                sl = j % 2
                t_w2l = P.dma("sy", r3(w2S if sl == 0 else w2S2, 128),
                              w2B[:, j * 128:(j + 1) * 128]
                              .rearrange("(c k) m -> k c m", k=128),
                              deps=[t_w2sl[sl],
                                    (toks_w2[-1] if (ch == 0 and j < 2) else None)])
                bank = 4 + j % 2

                def mk_pj(sl=sl, bank=bank):
                    def f(e):
                        wsrc = w2S if sl == 0 else w2S2
                        last = None
                        for c in range(FP):
                            last = e.matmul(
                                psum[bank][:], wsrc[:, c * 128:(c + 1) * 128],
                                harea[:, c * 512:(c + 1) * 512],
                                start=(c == 0), stop=(c == FP - 1))
                        return last
                    return f
                t_pmm = P.emit("te", mk_pj(),
                               deps=[t_w2l, mm_sq_last, t_proj_bank[bank]],
                               inc="te")
                t_w2sl[sl] = t_pmm
                t_projmm_prev = t_pmm

                def mk_out(j=j, bank=bank):
                    def f(e):
                        e.tensor_mul(psum[6][:], psum[bank][:], btile(2))
                        return e.tensor_add(xf[:, j * 512:(j + 1) * 512],
                                            psum[6][:],
                                            xf[:, j * 512:(j + 1) * 512])
                    return f
                t_oev = P.emit("ve", mk_out(), deps=[t_pmm, t_bF, t_xm], inc="ve")
                t_proj_bank[bank] = t_oev
            t_out_prev = P.dma("ac", outTv[:, :, colsl], r3(xf[:, :], 512),
                               deps=[t_proj_bank[4], t_proj_bank[5]])

        P.emit("sy", lambda e: e.nop(), deps=[t_out_prev])

        def run_eng(ename, eng):
            ths = P.thunks[ename]
            bounds = [(nm, idx[ename]) for nm, idx in P.marks] + [("end", len(ths))]
            pos = 0
            for (nm, start), (_, end) in zip(bounds, bounds[1:]):
                assert start == pos, (ename, nm, start, pos)
                with nc.named_scope(f"{nm}_{ename}"):
                    for th in ths[start:end]:
                        th(eng)
                pos = end
            for th in ths[pos:]:
                th(eng)

        with nc.Block() as blk_:
            @blk_.tensor
            def _(eng):
                run_eng("te", eng)

            @blk_.vector
            def _(eng):
                run_eng("ve", eng)

            @blk_.scalar
            def _(eng):
                run_eng("ac", eng)

            @blk_.gpsimd
            def _(eng):
                run_eng("gp", eng)

            @blk_.sync
            def _(eng):
                run_eng("sy", eng)

    return nc


_NC_CACHE = None


def kernel(L0_attn_hidden_states, L0_attn_Wi, L0_attn_Wf, L0_attn_Wg,
           L0_attn_Wo, L0_attn_gnorm_w, L0_ffn_W_fc, L0_ffn_W_proj):
    global _NC_CACHE
    x = np.ascontiguousarray(np.asarray(L0_attn_hidden_states))
    B = x.shape[0]
    wiT = np.ascontiguousarray(np.asarray(L0_attn_Wi).T)
    wfT = np.ascontiguousarray(np.asarray(L0_attn_Wf).T)
    wgT = np.ascontiguousarray(np.asarray(L0_attn_Wg).T)
    woT = np.ascontiguousarray(np.asarray(L0_attn_Wo).T)
    wfcT = np.ascontiguousarray(np.asarray(L0_ffn_W_fc).T)
    wprT = np.ascontiguousarray(np.asarray(L0_ffn_W_proj).T)
    gnw = np.ascontiguousarray(np.asarray(L0_attn_gnorm_w))

    if _NC_CACHE is None:
        _NC_CACHE = build_program()
    nc = _NC_CACHE

    in_maps = []
    for b in range(B):
        in_maps.append(dict(
            xT=np.ascontiguousarray(x[b].T),
            wiT=wiT, wfT=wfT, wgT=wgT, woT=woT,
            wfcT=wfcT, wprT=wprT, gnw=gnw,
        ))
    res = run_bass_kernel_spmd(nc, in_maps, core_ids=list(range(B)))
    out = np.empty_like(x)
    for b in range(B):
        out[b] = res.results[b]["outT"].T
    return out


# version 05
# speedup vs baseline: 1.0307x; 1.0307x over previous
import numpy as np
from contextlib import ExitStack

import concourse.bass as bass
import concourse.mybir as mybir
from concourse.bass_utils import run_bass_kernel_spmd

F32 = mybir.dt.float32
BF16 = mybir.dt.bfloat16
AF = mybir.ActivationFunctionType
ALU = mybir.AluOpType

H = 2048
HP = H // 128
F = 8192
FP = F // 128
T = 4096
NB = T // 512
NQ = 4
EPS_LN = 1e-5
EPS_RMS = 1e-5
EPS_FFN = float(np.finfo(np.float32).eps)
INV_C = 1.0 / 0.501953125

ENGS = ("te", "ve", "ac", "gp", "sy")
DEBUG_OUTPUTS = False


class Prog:

    def __init__(self, nc):
        self.nc = nc
        self.thunks = {e: [] for e in ENGS}
        self.sems = {}
        self.cnt = {}
        self.waited = {e: {} for e in ENGS}
        self.ss_sem = {}
        self.ss_ctr = {"ve": 0, "ac": 0}
        self.marks = []

    def mark(self, name):
        self.marks.append((name, {e: len(self.thunks[e]) for e in ENGS}))

    def add_sem(self, name, handle):
        self.sems[name] = handle
        self.cnt[name] = 0

    def _needed_waits(self, eng, deps):
        out = []
        for d in deps:
            if d is None:
                continue
            s, v = d
            if self.waited[eng].get(s, 0) < v:
                out.append((s, v))
                self.waited[eng][s] = v
        return out

    def emit(self, eng, fn, deps=(), inc=None, inc_by=1, is_dma=False):
        waits = self._needed_waits(eng, deps)
        tok = None
        if inc is not None:
            self.cnt[inc] += inc_by
            tok = (inc, self.cnt[inc])
        sems = self.sems
        psync = inc is not None and eng in ("ve", "ac") and not is_dma

        def thunk(engine, waits=waits, fn=fn, inc=inc, inc_by=inc_by, psync=psync):
            for s, v in waits:
                engine.wait_ge(sems[s], v)
            last = fn(engine)
            if inc is not None:
                assert last is not None
                if psync:
                    ss = self.ss_sem[eng]
                    self.ss_ctr[eng] += 1
                    last.then_inc(ss, 1)
                    engine.wait_ge(ss, self.ss_ctr[eng])
                    last = engine.engine_nop() if eng == "ve" else engine.nop()
                last.then_inc(sems[inc], inc_by)
        self.thunks[eng].append(thunk)
        return tok

    def dma(self, qeng, out, in_, deps=(), **kw):
        qsem = {"sy": "qs", "ac": "qa", "gp": "qg"}[qeng]
        return self.emit(qeng, lambda e: e.dma_start(out=out, in_=in_, **kw),
                         deps=deps, inc=qsem, inc_by=16, is_dma=True)


def build_program():
    nc = bass.Bass("TRN2", target_bir_lowering=False)

    xT = nc.dram_tensor("xT", [H, T], F32, kind="ExternalInput").ap()
    wiT = nc.dram_tensor("wiT", [H, H], F32, kind="ExternalInput").ap()
    wfT = nc.dram_tensor("wfT", [H, H], F32, kind="ExternalInput").ap()
    wgT = nc.dram_tensor("wgT", [H, H], F32, kind="ExternalInput").ap()
    woT = nc.dram_tensor("woT", [H, H], F32, kind="ExternalInput").ap()
    wfcT = nc.dram_tensor("wfcT", [H, F], F32, kind="ExternalInput").ap()
    wprT = nc.dram_tensor("wprT", [F, H], F32, kind="ExternalInput").ap()
    gnw = nc.dram_tensor("gnw", [H], F32, kind="ExternalInput").ap()
    outT = nc.dram_tensor("outT", [H, T], F32, kind="ExternalOutput").ap()
    _K = "ExternalOutput" if DEBUG_OUTPUTS else "Internal"
    oSp = nc.dram_tensor("oSp", [128, HP, T], BF16, kind=_K).ap()
    gsSp = nc.dram_tensor("gsSp", [128, HP, T], BF16, kind=_K).ap()
    xmid = nc.dram_tensor("xmid", [128, HP, T], F32, kind=_K).ap()
    w1B = nc.dram_tensor("w1B", [H, F], BF16, kind=_K).ap()
    w2B = nc.dram_tensor("w2B", [F, H], BF16, kind=_K).ap()
    rowsD = nc.dram_tensor("rowsD", [4, T], F32, kind=_K).ap()
    scD = nc.dram_tensor("scD", [32, 512], BF16, kind="Internal").ap()

    es = ExitStack()
    with es:
        def sb(name, shape, dt):
            return es.enter_context(nc.sbuf_tensor(name, shape, dt))

        arA = sb("arA", [128, 8192], F32)
        arB = sb("arB", [128, 16384], F32)
        arC = sb("arC", [128, 8192], F32)
        arD = sb("arD", [128, 4096], F32)
        arE = sb("arE", [128, 2048], F32)
        arF = sb("arF", [128, 2048], F32)
        scanT = sb("scanT", [128, 128], BF16)
        scanS = sb("scanS", [128, 128], BF16)
        scanS2 = sb("scanS2", [128, 128], BF16)
        scanP = sb("scanP", [128, 128], BF16)
        scanP2 = sb("scanP2", [128, 128], BF16)
        scanC = sb("scanC", [128, 128], BF16)
        gsT = sb("gsT", [128, 1024], BF16)
        sqP0 = sb("sqP0", [128, 2048], BF16)
        wslA = sb("wslA", [128, 6144], BF16)
        wslB = sb("wslB", [128, 6144], BF16)
        ones1 = sb("ones1", [128, 1], BF16)
        e256 = sb("e256", [128, 256], BF16)
        gncol = sb("gncol", [128, 16], F32)
        qcar = sb("qcar", [128, 16], BF16)
        rowW = sb("rowW", [1, 3584], F32)

        psum = [es.enter_context(nc.psum_tensor(f"pb{i}", [128, 512], F32))
                for i in range(8)]

        P = Prog(nc)
        for name in ("qs", "qa", "qg", "te", "ve", "ac"):
            P.add_sem(name, es.enter_context(nc.semaphore(name)))
        vs_sem = es.enter_context(nc.semaphore("vs"))
        as_sem = es.enter_context(nc.semaphore("as"))
        P.ss_sem = {"ve": vs_sem, "ac": as_sem}

        def ssv(engine, inst):
            P.ss_ctr["ve"] += 1
            inst.then_inc(vs_sem, 1)
            engine.wait_ge(vs_sem, P.ss_ctr["ve"])
            return inst

        aA = arA[:].bitcast(BF16)
        aB = arB[:].bitcast(BF16)
        aD = arD[:].bitcast(BF16)
        aE = arE[:].bitcast(BF16)
        xf = arC

        def btile(i):
            return arF[:, i * 512:(i + 1) * 512]

        def W(i):
            return rowW[0:1, i * 512:(i + 1) * 512]

        def rbc(lane, blk):
            return rowsD[lane:lane + 1, blk * 512:(blk + 1) * 512] \
                .unsqueeze(1).broadcast_to([1, 128, 512])

        def sbc(w_ap):
            return w_ap.unsqueeze(1).broadcast_to([1, 128, 512])

        xTv = xT.rearrange("(c k) t -> k c t", k=128)
        outTv = outT.rearrange("(c k) t -> k c t", k=128)

        def r3(ap, m):
            return ap.rearrange("p (c m) -> p c m", m=m)

        P.mark("P0")
        def init_consts(e):
            e.memset(ones1[:], 1.0)
            e.memset(e256[:], 0.0)
            e.memset(e256[0:64, 126:127], 1.0)
            e.memset(e256[64:128, 127:128], 1.0)
            e.memset(qcar[:], 0.0)
            return None
        P.emit("ve", init_consts)
        t_gn = P.dma("gp", gncol[:], gnw.rearrange("(c k) -> k c", k=128),
                     allow_slow_non_contiguous=True)

        def emit_rsqrt(var_ap, out_ap, t1_ap, deps):
            ta = P.emit("ac", lambda e: e.activation(t1_ap, var_ap, AF.Sqrt),
                        deps=deps, inc="ac")

            def f(e):
                ssv(e, e.reciprocal(out_ap, t1_ap))
                ssv(e, e.tensor_mul(t1_ap, out_ap, out_ap))
                ssv(e, e.tensor_mul(t1_ap, t1_ap, var_ap))
                ssv(e, e.tensor_scalar(t1_ap, t1_ap, -0.5, 1.5, ALU.mult, ALU.add))
                ssv(e, e.tensor_mul(out_ap, out_ap, t1_ap))
                return e.engine_nop()
            return P.emit("ve", f, deps=[ta], inc="ve")

        xb = aD
        t_mm_prev = None
        t_ev_prev = None
        t_xld_prev = None
        t_rowsD1 = [None] * NB
        for blk in range(NB):
            t_xb = P.dma("gp", r3(xb[:, 0:8192], 512),
                         xTv[:, :, blk * 512:(blk + 1) * 512],
                         deps=[t_mm_prev])
            p0_mm = {}
            for g in range(8):
                hh = g % 2
                t_sqh = P.emit(
                    "ac",
                    lambda e, g=g, hh=hh: e.activation(
                        sqP0[:, hh * 1024:(hh + 1) * 1024],
                        xb[:, g * 1024:(g + 1) * 1024], AF.Square),
                    deps=[t_xb, p0_mm.get(g - 2, t_mm_prev)], inc="ac")

                def mk_mm(g=g, hh=hh):
                    def f(e):
                        last = None
                        for c in range(2):
                            pt = g * 2 + c
                            e.matmul(psum[0][0:1, :], ones1[:],
                                     xb[:, pt * 512:(pt + 1) * 512],
                                     start=(pt == 0), stop=(pt == 15))
                            last = e.matmul(psum[1][0:1, :], ones1[:],
                                            sqP0[:, hh * 1024 + c * 512:
                                                 hh * 1024 + (c + 1) * 512],
                                            start=(pt == 0), stop=(pt == 15))
                        return last
                    return f
                t_mm_prev = P.emit("te", mk_mm(),
                                   deps=[t_sqh, t_ev_prev if g == 0 else None],
                                   inc="te")
                p0_mm[g] = t_mm_prev
            def p0rows(e):
                ssv(e, e.tensor_scalar_mul(W(0), psum[0][0:1, :], 1.0 / H))
                ssv(e, e.tensor_scalar_mul(W(1), psum[1][0:1, :], 1.0 / H))
                ssv(e, e.tensor_mul(W(4), W(0), W(0)))
                ssv(e, e.tensor_sub(W(1), W(1), W(4)))
                ssv(e, e.tensor_scalar_add(W(1), W(1), EPS_LN))
                return e.engine_nop()
            t_ev = P.emit("ve", p0rows,
                deps=[t_mm_prev, t_rowsD1[blk - 1] if blk else None], inc="ve")
            t_ev_prev = t_ev
            t_r1 = emit_rsqrt(W(1), W(2), W(4), [t_ev])
            t_mr = P.emit("ve", lambda e: (ssv(e, e.tensor_mul(W(3), W(0), W(2))),
                                           e.engine_nop())[-1],
                          deps=[t_r1], inc="ve")
            tok_a = P.dma("ac", rowsD[0:1, blk * 512:(blk + 1) * 512], W(2),
                          deps=[t_r1])
            tok_b = P.dma("ac", rowsD[1:2, blk * 512:(blk + 1) * 512], W(3),
                          deps=[t_mr])
            t_rowsD1[blk] = tok_b
        t_rstd1 = t_rowsD1[NB - 1]
        t_mr1 = t_rstd1

        P.mark("A")
        x1q = aA
        gate = aB
        vA = aB
        wslT = (wslA[:], wslB[:])
        t_x1_blk = [None, None]
        t_x1_last = None
        t_lastmm = None
        t_spill_o = [None, None]
        t_scan_grp = [None, None]
        t_gs_spill = None
        t_bank = {}
        t_tmpwar = {0: None, 1: None}
        t_slotmm = {}
        t_nblk_mm = {}
        scan_pending = []
        scan_last = [None]

        def drain_scan(k):
            while k > 0 and scan_pending:
                item = scan_pending.pop(0)
                if item[0] == "task":
                    scan_last[0] = P.emit("ve", item[1], deps=item[2], inc="ve")
                else:
                    _, g8, qq = item
                    t_spill_o[g8] = P.dma(
                        "ac", oSp[:, g8 * 8:(g8 + 1) * 8, qq * 1024:(qq + 1) * 1024],
                        vA[:, 16384 + g8 * 8192: 16384 + (g8 + 1) * 8192]
                        .rearrange("p (pt t) -> p pt t", t=1024),
                        deps=[scan_last[0]])
                k -= 1
        wsrcs = (wfT, wiT, wgT)

        for q in range(NQ):
            x1_toks = []
            for n in range(2):
                blk = q * 2 + n
                t_x = P.dma("sy", r3(xf[:, :], 512),
                            xTv[:, :, blk * 512:(blk + 1) * 512],
                            deps=[x1_toks[-1] if x1_toks else t_x1_last,
                                  t_rowsD1[blk] if q == 0 else None])
                t_b1 = P.dma("ac", btile(n * 2), rbc(0, blk),
                             deps=[t_rowsD1[blk], t_x1_last])
                t_b2 = P.dma("ac", btile(n * 2 + 1), rbc(1, blk),
                             deps=[t_rowsD1[blk], t_x1_last])

                def mk_x1(n):
                    def f(e):
                        last = None
                        for pt in range(HP):
                            e.tensor_mul(psum[6][:], xf[:, pt * 512:(pt + 1) * 512],
                                         btile(n * 2))
                            last = e.tensor_sub(
                                x1q[:, pt * 1024 + n * 512: pt * 1024 + n * 512 + 512],
                                psum[6][:], btile(n * 2 + 1))
                        return last
                    return f
                tok = P.emit("ve", mk_x1(n), deps=[t_x, t_b1, t_b2,
                                                   t_nblk_mm.get(n)],
                             inc="ve")
                x1_toks.append(tok)
                t_x1_blk[n] = tok
                t_x1_last = tok

            def load_wsl(po):
                slot = po % 2
                toks = []
                for wi, wsrc in enumerate(wsrcs):
                    toks.append(P.dma(
                        "gp", r3(wslT[slot][:, wi * 2048:(wi + 1) * 2048], 128),
                        wsrc[:, po * 128:(po + 1) * 128]
                        .rearrange("(c k) m -> k c m", k=128),
                        deps=[t_slotmm.get(slot)]))
                return toks
            if q == 0:
                wsl_next = load_wsl(0)
            for po in range(HP):
                grp = po // 8
                wsl_toks = wsl_next
                if po < HP - 1:
                    wsl_next = load_wsl(po + 1)
                elif q < NQ - 1:
                    wsl_next = load_wsl(0)
                for n in range(2):
                    bf_, bi_, bg_ = n % 2, 2 + n % 2, 4 + n % 2

                    def mk_fig(n=n, bf_=bf_, bi_=bi_, bg_=bg_, slot=po % 2):
                        def f(e):
                            last = None
                            for wi, bank in ((0, bf_), (1, bi_), (2, bg_)):
                                for c in range(HP):
                                    last = e.matmul(
                                        psum[bank][:],
                                        wslT[slot][:, wi * 2048 + c * 128:
                                                   wi * 2048 + (c + 1) * 128],
                                        x1q[:, c * 1024 + n * 512:
                                            c * 1024 + n * 512 + 512],
                                        start=(c == 0), stop=(c == HP - 1))
                            return last
                        return f
                    t_mm = P.emit("te", mk_fig(),
                                  deps=wsl_toks + [t_x1_blk[n], t_bank.get(bf_),
                                                   t_bank.get(bi_), t_bank.get(bg_)],
                                  inc="te")
                    t_lastmm = t_mm
                    t_nblk_mm[n] = t_mm
                    gm_ = aE[:, (n * 4 + 0) * 512:(n * 4 + 1) * 512]
                    sgi_ = aE[:, (n * 4 + 1) * 512:(n * 4 + 2) * 512]
                    sgg_ = aE[:, (n * 4 + 2) * 512:(n * 4 + 3) * 512]
                    si_ = aE[:, (n * 4 + 3) * 512:(n * 4 + 4) * 512]
                    g_dst = gate[:, po * 1024 + n * 512: po * 1024 + n * 512 + 512]
                    v_dst = vA[:, 16384 + po * 1024 + n * 512:
                               16384 + po * 1024 + n * 512 + 512]

                    def mk_act(bf_=bf_, bi_=bi_, bg_=bg_, g_dst=g_dst,
                               gm_=gm_, sgi_=sgi_, sgg_=sgg_):
                        def f(e):
                            e.activation(g_dst, psum[bf_][:], AF.Sigmoid)
                            e.activation(gm_, psum[bf_][:], AF.Sigmoid, scale=-1.0)
                            e.activation(sgi_, psum[bi_][:], AF.Sigmoid)
                            return e.activation(sgg_, psum[bg_][:], AF.Sigmoid)
                        return f
                    t_act = P.emit("ac", mk_act(),
                                   deps=[t_mm, t_spill_o[grp], t_tmpwar[n]],
                                   inc="ac")
                    t_bank[bf_] = t_act

                    def mk_vep(bi_=bi_, bg_=bg_, si_=si_, sgi_=sgi_, gm_=gm_,
                               sgg_=sgg_, v_dst=v_dst, n=n):
                        def f(e):
                            e.tensor_mul(si_, sgi_, psum[bi_][:])
                            e.tensor_mul(v_dst, si_, gm_)
                            return e.tensor_mul(gsT[:, n * 512:(n + 1) * 512],
                                                sgg_, psum[bg_][:])
                        return f
                    t_vep = P.emit("ve", mk_vep(),
                                   deps=[t_act, t_spill_o[grp], t_gs_spill],
                                   inc="ve")
                    t_bank[bi_] = t_vep
                    t_bank[bg_] = t_vep
                    t_tmpwar[n] = t_vep
                    t_bank[("g", po, n)] = t_act
                    t_bank[("v", po, n)] = t_vep
                    drain_scan(1)
                t_gs_spill = P.dma("ac", gsSp[:, po, q * 1024:(q + 1) * 1024],
                                   gsT[:, 0:1024], deps=[t_bank[5], t_bank[4]])
                t_slotmm[po % 2] = t_lastmm

                if po % 8 == 7:
                    g8 = grp
                    deps0 = []
                    for pp in range(g8 * 8, g8 * 8 + 8):
                        for n in range(2):
                            deps0 += [t_bank[("g", pp, n)], t_bank[("v", pp, n)]]
                    gate_g = gate[:, g8 * 8192:(g8 + 1) * 8192] \
                        .rearrange("p (pt m j) -> p pt m j", m=16, j=64)
                    v_g = vA[:, 16384 + g8 * 8192: 16384 + (g8 + 1) * 8192] \
                        .rearrange("p (pt m j) -> p pt m j", m=16, j=64)
                    tS = scanT[:].rearrange("p (pt m) -> p pt m", m=16)

                    def mk_l1(j0, j1, gate_g=gate_g, v_g=v_g):
                        def f(e):
                            last = None
                            for j in range(j0, j1):
                                e.tensor_mul(tS, gate_g[:, :, :, j],
                                             v_g[:, :, :, j - 1])
                                e.tensor_add(v_g[:, :, :, j], tS, v_g[:, :, :, j])
                                last = e.tensor_mul(gate_g[:, :, :, j],
                                                    gate_g[:, :, :, j],
                                                    gate_g[:, :, :, j - 1])
                            return last
                        return f

                    def mk_l2(gate_g=gate_g, v_g=v_g, g8=g8):
                        def f(e):
                            S = scanS[:].rearrange("p (pt m) -> p pt m", m=16)
                            S2 = scanS2[:].rearrange("p (pt m) -> p pt m", m=16)
                            Pp = scanP[:].rearrange("p (pt m) -> p pt m", m=16)
                            P2 = scanP2[:].rearrange("p (pt m) -> p pt m", m=16)
                            Ci = scanC[:].rearrange("p (pt m) -> p pt m", m=16)
                            e.tensor_copy(S, v_g[:, :, :, 63])
                            e.tensor_copy(Pp, gate_g[:, :, :, 63])
                            cur, nxt = (S, Pp), (S2, P2)
                            for s in (1, 2, 4, 8):
                                cS, cP = cur
                                nS, nP = nxt
                                ssv(e, e.tensor_mul(nS[:, :, s:], cP[:, :, s:],
                                                    cS[:, :, :16 - s]))
                                ssv(e, e.tensor_add(nS[:, :, s:], nS[:, :, s:],
                                                    cS[:, :, s:]))
                                ssv(e, e.tensor_copy(nS[:, :, 0:s], cS[:, :, 0:s]))
                                ssv(e, e.tensor_mul(nP[:, :, s:], cP[:, :, s:],
                                                    cP[:, :, :16 - s]))
                                ssv(e, e.tensor_copy(nP[:, :, 0:s], cP[:, :, 0:s]))
                                cur, nxt = nxt, cur
                            cS, cP = cur
                            nS, _ = nxt
                            for pp in range(8):
                                ssv(e, e.scalar_tensor_tensor(
                                    nS[:, pp, :], cP[:, pp, :],
                                    qcar[:, g8 * 8 + pp: g8 * 8 + pp + 1],
                                    cS[:, pp, :], ALU.mult, ALU.add))
                            ssv(e, e.tensor_copy(Ci[:, :, 0:1],
                                qcar[:, g8 * 8: g8 * 8 + 8].unsqueeze(2)))
                            ssv(e, e.tensor_copy(Ci[:, :, 1:16], nS[:, :, 0:15]))
                            ssv(e, e.tensor_copy(qcar[:, g8 * 8: g8 * 8 + 8],
                                                 nS[:, :, 15]))
                            return e.engine_nop()
                        return f

                    def mk_apply(j0, j1, gate_g=gate_g, v_g=v_g):
                        Ci = scanC[:].rearrange("p (pt m) -> p pt m", m=16)

                        def f(e):
                            last = None
                            for j in range(j0, j1):
                                e.tensor_mul(tS, gate_g[:, :, :, j], Ci)
                                last = e.tensor_add(v_g[:, :, :, j],
                                                    v_g[:, :, :, j], tS)
                            return last
                        return f

                    first = True
                    for j0 in range(1, 64, 9):
                        scan_pending.append(
                            ("task", mk_l1(j0, min(j0 + 9, 64)),
                             deps0 if first else []))
                        first = False
                    scan_pending.append(("task", mk_l2(), []))
                    for j0 in range(0, 64, 16):
                        scan_pending.append(("task", mk_apply(j0, j0 + 16), []))
                    scan_pending.append(("spill", g8, q))

        drain_scan(10 ** 9)
        tA = [t_lastmm, t_spill_o[0], t_spill_o[1], t_gs_spill]

        P.mark("B")
        oB = aA
        gsB = aA
        woS = aB
        o2T = aD
        t_wo = P.dma("gp", r3(woS[:, 0:32768], 2048),
                     woT.rearrange("(c k) m -> k c m", k=128), deps=tA)
        t_attn_bank = {2: None, 3: None}
        t_sqev_prev = None
        t_sq_prev = None
        t_o2_prev = None
        t_xm_spill = None
        t_ln2_ev_prev = None
        t_mm2_prev = None
        t_cst_prev = None
        t_b1_prev = None
        t_rowsD2 = [None] * NB
        sq_mm_toks = [None] * HP
        mm2_toks = {}
        sq2_toks = {}
        for blk in range(NB):
            mm2_toks.clear()
            t_o = P.dma("sy", r3(oB[:, 0:8192], 512),
                        oSp[:, :, blk * 512:(blk + 1) * 512],
                        deps=tA + [t_o2_prev, t_sq_prev])
            t_gs = P.dma("sy", r3(gsB[:, 8192:16384], 512),
                         gsSp[:, :, blk * 512:(blk + 1) * 512],
                         deps=[t_o2_prev])
            t_x = P.dma("sy", r3(xf[:, :], 512), xTv[:, :, blk * 512:(blk + 1) * 512],
                        deps=[t_x1_last, t_xm_spill])

            def mk_o2():
                def f(e):
                    last = None
                    for pt in range(HP):
                        last = e.scalar_tensor_tensor(
                            o2T[:, pt * 512:(pt + 1) * 512],
                            oB[:, pt * 512:(pt + 1) * 512],
                            gncol[:, pt:pt + 1],
                            gsB[:, 8192 + pt * 512: 8192 + (pt + 1) * 512],
                            ALU.mult, ALU.mult)
                    return last
                return f
            t_o2 = P.emit("ve", mk_o2(),
                          deps=[t_o, t_gs, t_gn, t_lastmm,
                                t_attn_bank[2], t_attn_bank[3]], inc="ve")
            t_o2_prev = t_o2
            sq_tiles = (aE[:, 0:512], aE[:, 512:1024], aE[:, 1024:1536],
                        aE[:, 1536:2048])
            sq_toks = [None] * HP
            mm_last = None
            for pt in range(HP + 3):
                if pt < HP:
                    d = sq_tiles[pt % 4]
                    sq_toks[pt] = P.emit(
                        "ac", lambda e, p=pt, d=d:
                        e.activation(d, oB[:, p * 512:(p + 1) * 512], AF.Square),
                        deps=[t_o, t_sq_prev if pt < 4 else sq_mm_toks[pt - 4],
                              t_mm2_prev if pt < 4 else None],
                        inc="ac")
                if pt >= 3:
                    p = pt - 3
                    mm_last = P.emit(
                        "te", lambda e, p=p, d=sq_tiles[p % 4]:
                        e.matmul(psum[6][0:1, :], ones1[:], d,
                                 start=(p == 0), stop=(p == HP - 1)),
                        deps=[sq_toks[p], (t_sqev_prev if p == 0 else None)],
                        inc="te")
                    sq_mm_toks[p] = mm_last
            t_sq_prev = mm_last
            t_sqev = P.emit("ve", lambda e:
                            (ssv(e, e.tensor_scalar(W(5), psum[6][0:1, :],
                                                    1.0 / H, EPS_RMS,
                                                    ALU.mult, ALU.add)),
                             e.engine_nop())[-1],
                            deps=[mm_last, t_b1_prev], inc="ve")
            t_sqev_prev = t_sqev
            t_rstd = emit_rsqrt(W(5), W(6), W(4), [t_sqev])
            t_b1 = P.dma("ac", btile(0), sbc(W(6)),
                         deps=[t_rstd, t_attn_bank[2], t_attn_bank[3]])
            t_b1_prev = t_b1
            for j in range(HP):
                bank = 2 + j % 2

                def mk_att(j=j, bank=bank):
                    def f(e):
                        last = None
                        for c in range(HP):
                            last = e.matmul(
                                psum[bank][:],
                                woS[:, c * 2048 + j * 128: c * 2048 + (j + 1) * 128],
                                o2T[:, c * 512:(c + 1) * 512],
                                start=(c == 0), stop=(c == HP - 1))
                        return last
                    return f
                t_mm = P.emit("te", mk_att(),
                              deps=[t_o2, t_wo, t_attn_bank[bank]], inc="te")

                def mk_ev(j=j, bank=bank):
                    def f(e):
                        e.tensor_mul(psum[7][:], psum[bank][:], btile(0))
                        return e.tensor_add(xf[:, j * 512:(j + 1) * 512],
                                            psum[7][:],
                                            xf[:, j * 512:(j + 1) * 512])
                    return f
                t_evj = P.emit("ve", mk_ev(), deps=[t_mm, t_x, t_b1], inc="ve")
                t_attn_bank[bank] = t_evj

                def emit_mm2(jj):
                    t_mm2 = P.emit("te", lambda e, j=jj, s=sq_tiles[jj % 2],
                                   d=sq_tiles[2 + jj % 2]: (
                        e.matmul(psum[4][0:1, :], ones1[:], s,
                                 start=(j == 0), stop=(j == HP - 1)),
                        e.matmul(psum[5][0:1, :], ones1[:], d,
                                 start=(j == 0), stop=(j == HP - 1)))[-1],
                        deps=[sq2_toks[jj],
                              (t_ln2_ev_prev if jj == 0 else None)], inc="te")
                    mm2_toks[jj] = t_mm2
                    return t_mm2

                if j >= 2:
                    t_mm2_prev = emit_mm2(j - 2)
                xm_t = sq_tiles[j % 2]
                sq_t = sq_tiles[2 + j % 2]
                t_cst = P.emit("ve", lambda e, j=j, d=xm_t:
                               e.tensor_copy(d, xf[:, j * 512:(j + 1) * 512]),
                               deps=[t_evj, mm2_toks.get(j - 2),
                                     (mm_last if j < 2 else None)], inc="ve")
                t_cst_prev = t_cst
                t_sq2 = P.emit("ac", lambda e, s=xm_t, d=sq_t:
                               e.activation(d, s, AF.Square),
                               deps=[t_cst, mm2_toks.get(j - 2),
                                     (mm_last if j < 2 else None)], inc="ac")
                sq2_toks[j] = t_sq2
                if j == HP - 1:
                    t_mm2_prev = emit_mm2(j - 1)
                    t_mm2_prev = emit_mm2(j)
            def ln2rows(e):
                ssv(e, e.tensor_scalar_mul(W(0), psum[4][0:1, :], 1.0 / H))
                ssv(e, e.tensor_scalar_mul(W(1), psum[5][0:1, :], 1.0 / H))
                ssv(e, e.tensor_mul(W(4), W(0), W(0)))
                ssv(e, e.tensor_sub(W(1), W(1), W(4)))
                ssv(e, e.tensor_scalar_add(W(1), W(1), EPS_LN))
                return e.engine_nop()
            t_ln2_ev = P.emit("ve", ln2rows,
                deps=[t_mm2_prev, t_rowsD2[blk - 1] if blk else t_rowsD1[NB - 1]],
                inc="ve")
            t_ln2_ev_prev = t_ln2_ev
            t_r2 = emit_rsqrt(W(1), W(2), W(4), [t_ln2_ev])
            t_mr2b = P.emit("ve", lambda e: (ssv(e, e.tensor_copy(W(3), W(0))),
                                             e.engine_nop())[-1],
                            deps=[t_r2], inc="ve")
            tok_a = P.dma("ac", rowsD[2:3, blk * 512:(blk + 1) * 512], W(2),
                          deps=[t_r2])
            tok_b = P.dma("ac", rowsD[3:4, blk * 512:(blk + 1) * 512], W(3),
                          deps=[t_mr2b])
            t_rowsD2[blk] = tok_b
            t_xm_spill = P.dma("ac", xmid[:, :, blk * 512:(blk + 1) * 512],
                               r3(xf[:, :], 512),
                               deps=[t_attn_bank[2], t_attn_bank[3], t_cst_prev])
        t_rstd2 = t_rowsD2[NB - 1]
        t_mr2 = t_rstd2

        P.mark("Q")
        aC = arC[:].bitcast(BF16)
        panels = []
        for pg in range(HP // 16):
            for cb in range(F // 512):
                panels.append((wfcT, w1B, F, pg, cb))
        for pg in range(FP // 16):
            for cb in range(H // 512):
                panels.append((wprT, w2B, H, pg, cb))
        nP = len(panels)
        scRow = aE[0:32, 0:512]
        dep0 = [t_xm_spill, t_sq_prev, t_mm2_prev]
        q_abs = [None] * nP
        q_sgn = [None] * nP
        q_gsm = [None] * nP
        q_ev = [None] * nP
        q_bc = [None] * nP
        q_ms = [None] * nP
        q_w1 = [None] * nP
        q_st = [None] * nP
        q_scw = [None] * nP

        def q_stage1(i):
            srcT, dstB, nCols, pg, cb = panels[i]
            s = i % 2
            wb_t = aB[:, 16384 + s * 8192: 16384 + (s + 1) * 8192]
            awb_t = aC[:, s * 8192:(s + 1) * 8192]
            sgn_t = aA[:, s * 8192:(s + 1) * 8192]
            colsl = slice(cb * 512, (cb + 1) * 512)
            t_wb = P.dma("gp", r3(wb_t, 512),
                         srcT[pg * 2048:(pg + 1) * 2048, colsl]
                         .rearrange("(c k) m -> k c m", k=128),
                         deps=dep0 + [q_sgn[i - 2] if i >= 2 else None])
            q_abs[i] = P.emit("ac", lambda e, d=awb_t, sc=wb_t:
                              e.activation(d, sc, AF.Abs),
                              deps=dep0 + [t_wb,
                                           q_st[i - 2] if i >= 2 else None],
                              inc="ac")
            q_sgn[i] = P.emit("ac", lambda e, d=sgn_t, sc=wb_t:
                              e.activation(d, sc, AF.Sign),
                              deps=[t_wb, q_w1[i - 2] if i >= 2 else None],
                              inc="ac")

            def mk_gs(awb_t=awb_t):
                def f(e):
                    last = None
                    for p in range(16):
                        sx = 126 - 2 * p
                        last = e.matmul(psum[7][0:32, :], e256[:, sx:sx + 32],
                                        awb_t[:, p * 512:(p + 1) * 512],
                                        start=(p == 0), stop=(p == 15))
                    return last
                return f
            q_gsm[i] = P.emit("te", mk_gs(),
                              deps=[q_abs[i], q_ev[i - 1] if i else None],
                              inc="te")

        def q_stage2(i):
            s = i % 2
            scb_t = aB[:, s * 8192:(s + 1) * 8192]
            q_ev[i] = P.emit("ve", lambda e:
                             (ssv(e, e.tensor_scalar(scRow, psum[7][0:32, :],
                                                     1.0 / 64.0, 1e-8,
                                                     ALU.mult, ALU.max)),
                              e.engine_nop())[-1],
                             deps=dep0 + [q_gsm[i],
                                          q_scw[i - 1] if i else None], inc="ve")
            q_scw[i] = P.dma("sy", scD[:, :], scRow,
                             deps=[q_ev[i], q_bc[i - 1] if i else None])
            bcs = []
            for hh in range(2):
                srcv = scD.rearrange("(p h) m -> h p m", h=2)[hh] \
                    .unsqueeze(0).broadcast_to([64, 16, 512])
                bcs.append(P.dma("sy", scb_t[64 * hh:64 * hh + 64, :], srcv,
                                 deps=dep0 + [q_scw[i],
                                              q_ms[i - 2] if i >= 2 else None]))
            q_bc[i] = bcs[-1]

        def q_stage3(i):
            srcT, dstB, nCols, pg, cb = panels[i]
            s = i % 2
            awb_t = aC[:, s * 8192:(s + 1) * 8192]
            sgn_t = aA[:, s * 8192:(s + 1) * 8192]
            scb_t = aB[:, s * 8192:(s + 1) * 8192]
            colsl = slice(cb * 512, (cb + 1) * 512)
            t_mask = P.emit("ve", lambda e, a=awb_t, sc=scb_t:
                            e.scalar_tensor_tensor(a, a, INV_C, sc,
                                                   ALU.mult, ALU.is_gt),
                            deps=[q_bc[i]], inc="ve")
            q_ms[i] = P.emit("ve", lambda e, a=awb_t, sc=scb_t:
                             e.tensor_mul(a, a, sc),
                             deps=[t_mask], inc="ve")
            q_w1[i] = P.emit("ve", lambda e, a=awb_t, g=sgn_t:
                             e.tensor_mul(a, a, g),
                             deps=[q_ms[i], q_sgn[i]], inc="ve")
            q_st[i] = P.dma("ac",
                            dstB[pg * 2048:(pg + 1) * 2048, colsl]
                            .rearrange("(c k) m -> k c m", k=128),
                            r3(awb_t, 512), deps=[q_w1[i]])

        for i in range(nP):
            q_stage1(i)
            if i >= 1:
                q_stage3(i - 1)
            q_stage2(i)
        q_stage3(nP - 1)
        toks_w1 = [q_st[HP // 16 * (F // 512) - 1]]
        toks_w2 = [q_st[nP - 1]]

        P.mark("C")
        x2T = aD
        harea = aB
        w2S = aA[:, 0:8192]
        w2S2 = aA[:, 8192:16384]
        rT, rT2 = aE[:, 0:512], aE[:, 512:1024]
        sqC1, sqC2 = aE[:, 1024:1536], aE[:, 1536:2048]
        t_h_bank = {0: None, 1: None, 2: None}
        t_proj_bank = {4: None, 5: None}
        t_projmm_prev = None
        t_w1sl = [None, None, None]
        t_w2sl = [None, None]
        t_x2_prev = None
        t_hs_prev = None
        t_out_prev = None
        t_bF_prev = None
        t_rT_war = {0: None, 1: None}
        t_sqC_war = {0: None, 1: None}
        sq4_toks = {}
        t_hmm_last = None
        for ch in range(NB):
            colsl = slice(ch * 512, (ch + 1) * 512)
            t_xmb = P.dma("gp", r3(x2T[:, 0:8192], 512), xmid[:, :, colsl],
                          deps=[toks_w2[-1] if ch == 0 else None, t_hmm_last])
            t_xm = P.dma("sy", r3(xf[:, :], 512), xmid[:, :, colsl],
                         deps=[toks_w2[-1] if ch == 0 else None, t_out_prev])
            t_b1 = P.dma("ac", btile(0), rbc(2, ch),
                         deps=[t_rowsD2[ch], t_x2_prev])
            t_b2 = P.dma("ac", btile(1), rbc(3, ch),
                         deps=[t_rowsD2[ch], t_x2_prev])

            def mk_x2():
                def f(e):
                    last = None
                    for pt in range(HP):
                        sl_ = x2T[:, pt * 512:(pt + 1) * 512]
                        e.tensor_sub(sl_, sl_, btile(1))
                        last = e.tensor_mul(sl_, sl_, btile(0))
                    return last
                return f
            t_x2 = P.emit("ve", mk_x2(),
                          deps=[t_xmb, t_b1, t_b2,
                                (toks_w2[-1] if ch == 0 else None), t_x2_prev],
                          inc="ve")
            t_x2_prev = t_x2

            mm_sq_last = None
            for qo in range(FP):
                sl = qo % 3
                t_wl = P.dma("sy", r3(wslA[:, sl * 2048:(sl + 1) * 2048], 128),
                             w1B[:, qo * 128:(qo + 1) * 128]
                             .rearrange("(c k) m -> k c m", k=128),
                             deps=[t_w1sl[sl], t_lastmm,
                                   (toks_w1[-1] if (ch == 0 and qo < 3) else None)])
                bank = qo % 3

                def mk_h(sl=sl, bank=bank):
                    def f(e):
                        last = None
                        for c in range(HP):
                            last = e.matmul(
                                psum[bank][:],
                                wslA[:, sl * 2048 + c * 128: sl * 2048 + (c + 1) * 128],
                                x2T[:, c * 512:(c + 1) * 512],
                                start=(c == 0), stop=(c == HP - 1))
                        return last
                    return f
                t_hmm = P.emit("te", mk_h(), deps=[t_wl, t_x2, t_h_bank[bank]],
                               inc="te")
                t_w1sl[sl] = t_hmm
                t_hmm_last = t_hmm

                def emit_msq(q):
                    dq = sqC1 if q % 2 == 0 else sqC2
                    tok = P.emit("te", lambda e, q=q, dq=dq:
                                 e.matmul(psum[3][0:1, :], ones1[:], dq,
                                          start=(q == 0), stop=(q == FP - 1)),
                                 deps=[sq4_toks[q],
                                       (t_hs_prev if q == 0 else None)],
                                 inc="te")
                    t_sqC_war[q % 2] = tok
                    return tok

                if qo >= 2:
                    mm_sq_last = emit_msq(qo - 2)
                d = rT if qo % 2 == 0 else rT2
                t_r = P.emit("ac", lambda e, b=bank, d=d:
                             e.activation(d, psum[b][:], AF.Relu),
                             deps=[t_hmm, t_rT_war[qo % 2]], inc="ac")
                t_h_bank[bank] = t_r
                t_hsq = P.emit("ve", lambda e, q=qo, d=d:
                               e.tensor_mul(harea[:, q * 512:(q + 1) * 512], d, d),
                               deps=[t_r, (t_projmm_prev if qo == 0 else None),
                                     (toks_w2[-1] if (ch == 0 and qo == 0) else None)],
                               inc="ve")
                t_rT_war[qo % 2] = t_hsq
                dq = sqC1 if qo % 2 == 0 else sqC2
                t_sq4 = P.emit("ac", lambda e, q=qo, dq=dq:
                               e.activation(dq, harea[:, q * 512:(q + 1) * 512],
                                            AF.Square),
                               deps=[t_hsq, t_sqC_war[qo % 2]], inc="ac")
                sq4_toks[qo] = t_sq4
                if qo == FP - 1:
                    emit_msq(qo - 1)
                    mm_sq_last = emit_msq(qo)
            t_ev = P.emit("ve", lambda e:
                          (ssv(e, e.tensor_scalar(W(5), psum[3][0:1, :],
                                                  1.0 / F, EPS_FFN,
                                                  ALU.mult, ALU.add)),
                           e.engine_nop())[-1],
                          deps=[mm_sq_last, t_bF_prev], inc="ve")
            t_hs_prev = t_ev
            t_rstdF = emit_rsqrt(W(5), W(6), W(4), [t_ev])
            t_bF = P.dma("ac", btile(2), sbc(W(6)), deps=[t_rstdF])
            t_bF_prev = t_bF
            for j in range(HP):
                sl = j % 2
                t_w2l = P.dma("sy", r3(w2S if sl == 0 else w2S2, 128),
                              w2B[:, j * 128:(j + 1) * 128]
                              .rearrange("(c k) m -> k c m", k=128),
                              deps=[t_w2sl[sl],
                                    (toks_w2[-1] if (ch == 0 and j < 2) else None)])
                bank = 4 + j % 2

                def mk_pj(sl=sl, bank=bank):
                    def f(e):
                        wsrc = w2S if sl == 0 else w2S2
                        last = None
                        for c in range(FP):
                            last = e.matmul(
                                psum[bank][:], wsrc[:, c * 128:(c + 1) * 128],
                                harea[:, c * 512:(c + 1) * 512],
                                start=(c == 0), stop=(c == FP - 1))
                        return last
                    return f
                t_pmm = P.emit("te", mk_pj(),
                               deps=[t_w2l, mm_sq_last, t_proj_bank[bank]],
                               inc="te")
                t_w2sl[sl] = t_pmm
                t_projmm_prev = t_pmm

                def mk_out(j=j, bank=bank):
                    def f(e):
                        e.tensor_mul(psum[6][:], psum[bank][:], btile(2))
                        return e.tensor_add(xf[:, j * 512:(j + 1) * 512],
                                            psum[6][:],
                                            xf[:, j * 512:(j + 1) * 512])
                    return f
                t_oev = P.emit("ve", mk_out(), deps=[t_pmm, t_bF, t_xm], inc="ve")
                t_proj_bank[bank] = t_oev
            t_out_prev = P.dma("ac", outTv[:, :, colsl], r3(xf[:, :], 512),
                               deps=[t_proj_bank[4], t_proj_bank[5]])

        P.emit("sy", lambda e: e.nop(), deps=[t_out_prev])

        def run_eng(ename, eng):
            ths = P.thunks[ename]
            bounds = [(nm, idx[ename]) for nm, idx in P.marks] + [("end", len(ths))]
            pos = 0
            for (nm, start), (_, end) in zip(bounds, bounds[1:]):
                assert start == pos, (ename, nm, start, pos)
                with nc.named_scope(f"{nm}_{ename}"):
                    for th in ths[start:end]:
                        th(eng)
                pos = end
            for th in ths[pos:]:
                th(eng)

        with nc.Block() as blk_:
            @blk_.tensor
            def _(eng):
                run_eng("te", eng)

            @blk_.vector
            def _(eng):
                run_eng("ve", eng)

            @blk_.scalar
            def _(eng):
                run_eng("ac", eng)

            @blk_.gpsimd
            def _(eng):
                run_eng("gp", eng)

            @blk_.sync
            def _(eng):
                run_eng("sy", eng)

    return nc


_NC_CACHE = None


def kernel(L0_attn_hidden_states, L0_attn_Wi, L0_attn_Wf, L0_attn_Wg,
           L0_attn_Wo, L0_attn_gnorm_w, L0_ffn_W_fc, L0_ffn_W_proj):
    global _NC_CACHE
    x = np.ascontiguousarray(np.asarray(L0_attn_hidden_states))
    B = x.shape[0]
    wiT = np.ascontiguousarray(np.asarray(L0_attn_Wi).T)
    wfT = np.ascontiguousarray(np.asarray(L0_attn_Wf).T)
    wgT = np.ascontiguousarray(np.asarray(L0_attn_Wg).T)
    woT = np.ascontiguousarray(np.asarray(L0_attn_Wo).T)
    wfcT = np.ascontiguousarray(np.asarray(L0_ffn_W_fc).T)
    wprT = np.ascontiguousarray(np.asarray(L0_ffn_W_proj).T)
    gnw = np.ascontiguousarray(np.asarray(L0_attn_gnorm_w))

    if _NC_CACHE is None:
        _NC_CACHE = build_program()
    nc = _NC_CACHE

    in_maps = []
    for b in range(B):
        in_maps.append(dict(
            xT=np.ascontiguousarray(x[b].T),
            wiT=wiT, wfT=wfT, wgT=wgT, woT=woT,
            wfcT=wfcT, wprT=wprT, gnw=gnw,
        ))
    res = run_bass_kernel_spmd(nc, in_maps, core_ids=list(range(B)))
    out = np.empty_like(x)
    for b in range(B):
        out[b] = res.results[b]["outT"].T
    return out
